# version 02
# baseline (speedup 1.0000x reference)
import numpy as np
import ml_dtypes
from contextlib import ExitStack

import concourse.bass as bass
import concourse.mybir as mybir
from concourse.bass_utils import run_bass_kernel_spmd
from concourse.alu_op_type import AluOpType as Op

F32 = mybir.dt.float32
BF16 = mybir.dt.bfloat16
U8 = mybir.dt.uint8
AF = mybir.ActivationFunctionType

B = 4
T = 4096
H = 4096
D = 8192
DH = 4096
N = 16
R = 256
CK = 4
OH = 2048
NL = 4
RK = 16
TQ = 1024
NTQ = T // TQ
FC = 512
NFC = T // FC
NDT = DH // 128
NKH = H // 128
NKD = D // 128
NMO = OH // 128
SSM = R + 2 * N
EPS_RMS = 1e-6
EPS_LN = 1e-5
GRPD = 2

DEBUG = False
NOCC = False
PESTRICT = False
STAGE = 4

ENGINES = ("sp", "gp", "pe", "act", "dve")
GROUPS = [[0, 1], [2, 3], [4, 5], [6, 7]]

_NC_CACHE = {}


class Prog:

    NLANE = 8

    def __init__(self):
        self.items = {e: [] for e in ENGINES}
        self.marks = {}
        self.dma_cnt = {e: 0 for e in ENGINES}
        self.last_by_sem = {e: {} for e in ENGINES}

    def add(self, eng, fn, waits=(), mark=None, dma=False, sem=None, inc=None):
        if sem is None:
            if dma:
                sem = f"{eng}_d{self.dma_cnt[eng] % self.NLANE}"
                self.dma_cnt[eng] += 1
            else:
                sem = eng
        if inc is None:
            inc = 16 if dma else 1
        idx = len(self.items[eng])
        self.items[eng].append((fn, tuple(waits), inc, sem))
        if sem is not None:
            self.last_by_sem[eng][sem] = idx
        if mark is not None:
            assert mark not in self.marks, mark
            self.marks[mark] = (eng, idx)
        return (eng, idx)

    def cur(self, eng):
        return (eng, len(self.items[eng]) - 1)

    def cur_all(self, eng):
        return [(eng, i) for i in self.last_by_sem[eng].values()]

    def alias(self, new, old):
        self.marks[new] = self.marks[old] if isinstance(old, str) else old

    strict_engines = ("dve", "act")

    def emit(self, block, sems):
        cum = {}
        for e in ENGINES:
            acc = {}
            lst = []
            for (_, _, inc, sem) in self.items[e]:
                acc[sem] = acc.get(sem, 0) + inc
                lst.append((sem, acc[sem]))
            cum[e] = lst

        def mval(m):
            e, i = self.marks[m] if isinstance(m, str) else m
            if i < 0:
                return e, None, 0
            sem, v = cum[e][i]
            return e, sem, v

        def runner(eng):
            def body(sec):
                wm = {}
                prev = 0
                lane_cum = {}
                for (fn, waits, inc, sem) in self.items[eng]:
                    if eng in self.strict_engines and prev > 0 and \
                            wm.get(eng, 0) < prev:
                        sec.wait_ge(sems[eng], prev)
                        wm[eng] = prev
                    if sem is not None and "_d" in str(sem):
                        before = lane_cum.get(sem, 0)
                        if before > 0 and wm.get(sem, 0) < before:
                            sec.wait_ge(sems[sem], before)
                            wm[sem] = before
                        lane_cum[sem] = before + inc
                    for m in waits:
                        pe_, psem, v = mval(m)
                        if psem is None or v <= 0:
                            continue
                        if pe_ == eng and psem == eng and \
                                eng in ("dve", "act", "pe"):
                            continue
                        if wm.get(psem, 0) < v:
                            sec.wait_ge(sems[psem], v)
                            wm[psem] = v
                    ins = fn(sec)
                    if sem is not None:
                        ins.then_inc(sems[sem], inc)
                        if sem == eng:
                            prev += inc
            return body

        block.sync(runner("sp"))
        block.gpsimd(runner("gp"))
        block.tensor(runner("pe"))
        block.scalar(runner("act"))
        block.vector(runner("dve"))


class Arena:
    def __init__(self, handle, total):
        self.h = handle
        self.total = total
        self.off = 0
        self.hwm = 0

    def alloc(self, parts, width):
        a = self.off
        self.off += width
        self.hwm = max(self.hwm, self.off)
        assert self.off <= self.total, f"arena overflow {self.off} > {self.total}"
        return self.h.ap()[0:parts, a:a + width]

    def save(self):
        return self.off

    def reset(self, to=0):
        self.off = to


def _lora_index_table():
    import jax
    import jax.numpy as jnp

    def f():
        u = jax.random.uniform(jax.random.key(42, impl="threefry2x32"),
                               (B, T, H))
        return jnp.floor(u * 4).astype(jnp.uint8)
    return np.asarray(jax.jit(f, backend="cpu")())


def build_nc(debug=False, stage=4, nocc=False, pestrict=False):
    nc = bass.Bass("TRN2", target_bir_lowering=False)
    dt = nc.dram_tensor

    xT = dt("xT", [H, T], F32, kind="ExternalInput").ap()
    x0my = dt("x0my", [OH, T], F32, kind="ExternalInput").ap()
    WuT = dt("WuT", [H, DH], F32, kind="ExternalInput").ap()
    WgT = dt("WgT", [H, DH], F32, kind="ExternalInput").ap()
    WxT = dt("WxT", [D, SSM], F32, kind="ExternalInput").ap()
    WdtT = dt("WdtT", [R, DH], F32, kind="ExternalInput").ap()
    WoT = dt("WoT", [DH, H], F32, kind="ExternalInput").ap()
    W0T = dt("W0T", [H, OH], F32, kind="ExternalInput").ap()
    lAT = dt("lAT", [H, NL * RK], F32, kind="ExternalInput").ap()
    lBT = dt("lBT", [NL * RK, OH], F32, kind="ExternalInput").ap()
    A_cols = dt("A_cols", [128, NDT * N], F32, kind="ExternalInput").ap()
    convw = dt("convw", [128, NDT * CK], F32, kind="ExternalInput").ap()
    cpack = dt("cpack", [128, 3 * NDT + 2 + NMO + 5], F32, kind="ExternalInput").ap()
    wstrip = dt("wstrip", [32, 32 * 128], F32, kind="ExternalInput").ap()
    selb = dt("selb", [1, 32], F32, kind="ExternalInput").ap()
    selc = dt("selc", [1, 32], F32, kind="ExternalInput").ap()
    onesb32 = dt("onesb32", [32, 2], F32, kind="ExternalInput").ap()
    ones128 = dt("ones128", [128, 2], F32, kind="ExternalInput").ap()
    onesrow = dt("onesrow", [1, 128], F32, kind="ExternalInput").ap()
    ident = dt("ident", [128, 128], F32, kind="ExternalInput").ap()
    lidx = dt("lidx", [OH, T], U8, kind="ExternalInput").ap()

    out_half = dt("out_half", [T, OH], F32, kind="ExternalOutput").ap()

    lvl = (2 if debug is True else int(debug or 0))
    debug = lvl >= 2
    dkind = "ExternalOutput" if debug else "Internal"
    u_own = [dt(f"u_own{q}", [DH, TQ], BF16, kind="Internal").ap()
             for q in range(NTQ)]
    u_gs = [[dt(f"u_g{q}_{s}", [4096, TQ], BF16, kind="Internal").ap()
             for s in range(2)] for q in range(NTQ)]
    g_own = dt("g_own", [DH, T], BF16, kind=dkind).ap()
    y_own = dt("y_own", [DH, T], BF16,
               kind=("ExternalOutput" if lvl >= 1 else "Internal")).ap()
    win_st = dt("win_st", [2 * NDT, 128, NKH * 128], BF16, kind="Internal").ap()
    wo_st = dt("wo_st", [DH, H], BF16, kind="Internal").ap()
    w0_st = dt("w0_st", [H, OH], BF16, kind="Internal").ap()
    attn_in = [dt(f"attn_in{i}", [H, FC], BF16, kind="Internal").ap() for i in range(2)]
    attn_rs = [dt(f"attn_rs{i}", [OH, FC], BF16, kind="Internal").ap() for i in range(2)]
    attn_ag = [dt(f"attn_ag{i}", [H, FC], BF16, kind="Internal").ap() for i in range(2)]
    stats_in = [dt(f"stats_in{i}", [2, FC], F32, kind="Internal").ap() for i in range(2)]
    stats_out = [dt(f"stats_out{i}", [2, FC], F32, kind="Internal").ap()
                 for i in range(2)]
    dbg = {}
    if lvl == 3:
        dbg["arena_bf"] = dt("dbg_arena_bf", [128, 61440], BF16,
                             kind="ExternalOutput").ap()
        dbg["arena_f32"] = dt("dbg_arena_f32", [128, 7168], F32,
                              kind="ExternalOutput").ap()
    if lvl >= 1:
        dbg["delta"] = dt("dbg_delta", [DH, T], BF16, kind="ExternalOutput").ap()
    if debug:
        dbg["ssm"] = dt("dbg_ssm", [SSM, T], F32, kind="ExternalOutput").ap()
        dbg["x1"] = dt("dbg_x1", [H, T], BF16, kind="ExternalOutput").ap()
        dbg["u"] = dt("dbg_u", [D, T], BF16, kind="ExternalOutput").ap()
        dbg["attn"] = dt("dbg_attn", [H, T], BF16, kind="ExternalOutput").ap()

    p = Prog()
    if pestrict:
        p.strict_engines = ("dve", "act", "pe")

    with ExitStack() as stack:
        e = stack.enter_context
        AB = Arena(e(nc.sbuf_tensor("arena_bf", [128, 61440], BF16)), 61440)
        AFA = Arena(e(nc.sbuf_tensor("arena_f32", [128, 7168], F32)), 7168)
        wrng = [e(nc.sbuf_tensor(f"wring{i}", [128, 4096], BF16)) for i in range(2)]
        wdtr = [e(nc.sbuf_tensor(f"wdtr{i}", [128, 256], BF16)) for i in range(2)]
        c_ident = e(nc.sbuf_tensor("c_ident", [128, 128], F32))
        c_cols = e(nc.sbuf_tensor("c_cols", [128, 3 * NDT + 2 + NMO + 5], F32))
        c_acols = e(nc.sbuf_tensor("c_acols", [128, NDT * N], F32))
        c_convw = e(nc.sbuf_tensor("c_convw", [128, NDT * CK], F32))
        c_wstrip = e(nc.sbuf_tensor("c_wstrip", [32, 32 * 128], BF16))
        c_selb = e(nc.sbuf_tensor("c_selb", [1, 32], BF16))
        c_selc = e(nc.sbuf_tensor("c_selc", [1, 32], BF16))
        c_ones32 = e(nc.sbuf_tensor("c_ones32", [32, 2], BF16))
        c_ones = e(nc.sbuf_tensor("c_ones", [128, 2], BF16))
        c_onesrow = e(nc.sbuf_tensor("c_onesrow", [1, 128], BF16))
        carry_u = e(nc.sbuf_tensor("carry_u", [128, NDT * (CK - 1)], F32))
        carry_s = e(nc.sbuf_tensor("carry_s", [128, NDT * N], F32))
        lA_sb = e(nc.sbuf_tensor("lA_sb", [128, NKH * NL * RK], BF16))
        lB_sb = [e(nc.sbuf_tensor(f"lB_sb{l}", [RK, OH], BF16)) for l in range(NL)]
        lid_sb = e(nc.sbuf_tensor("lid_sb", [128, NMO * FC], U8))
        psum = [e(nc.psum_tensor(f"ps{i}", [128, 512], F32)) for i in range(8)]
        sems = {n: e(nc.semaphore(n)) for n in ENGINES}
        for qn in ("sp", "gp"):
            for ln in range(Prog.NLANE):
                sems[f"{qn}_d{ln}"] = e(nc.semaphore(f"{qn}_d{ln}"))
        block = e(nc.Block())

        coff = {"convb": 0, "dtbias": NDT, "Dp": 2 * NDT, "dtlnw": 3 * NDT,
                "b0c": 3 * NDT + 2, "invR": 3 * NDT + 2 + NMO,
                "invN": 3 * NDT + 3 + NMO, "neghalf": 3 * NDT + 4 + NMO,
                "epsrms": 3 * NDT + 5 + NMO, "epsln": 3 * NDT + 6 + NMO}

        def col(name, j):
            o = coff[name] + j
            return c_cols.ap()[:, o:o + 1]

        def col1(name):
            o = coff[name]
            return c_cols.ap()[0:1, o:o + 1]

        cc_n = [0]

        def collective(kind, op, ins, outs, waits, mark):
            cc_n[0] += 1
            if not nocc:
                ccs = f"cc{cc_n[0]}"
                sems[ccs] = e(nc.semaphore(ccs))
            if nocc:
                iap, oap = ins[0], outs[0]
                ir = iap.shape[0]
                if oap.shape[0] == 2 * ir:
                    p.add("gp", lambda s: s.dma_start(out=oap[0:ir, :], in_=iap),
                          waits=waits, dma=True, mark=mark + "_a")
                    p.add("gp", lambda s: s.dma_start(out=oap[ir:2 * ir, :],
                                                      in_=iap),
                          waits=[mark + "_a"], dma=True, mark=mark)
                elif oap.shape[0] * 2 == ir:
                    hr = oap.shape[0]
                    p.add("gp", lambda s: s.dma_start(out=oap, in_=iap[0:hr, :]),
                          waits=waits, dma=True, mark=mark)
                else:
                    p.add("gp", lambda s: s.dma_start(out=oap, in_=iap),
                          waits=waits, dma=True, mark=mark)
                return

            def f(sec, kind=kind, op=op, ins=ins, outs=outs):
                return sec.collective_compute(
                    kind, op, replica_groups=GROUPS,
                    ins=[a.opt() for a in ins], outs=[a.opt() for a in outs])
            p.add("gp", f, waits=waits, mark=mark, sem=ccs, inc=1)

        for i_, (dst_, src_) in enumerate((
                (c_ident, ident), (c_cols, cpack), (c_acols, A_cols),
                (c_convw, convw))):
            p.add("sp", lambda s, d_=dst_, s_=src_: s.dma_start(out=d_[:], in_=s_),
                  dma=True, mark=f"ldc{i_}")
        LD_CONSTS = [f"ldc{i_}" for i_ in range(4)]
        for i_, (dst_, src_) in enumerate((
                (c_wstrip, wstrip), (c_selb, selb), (c_selc, selc),
                (c_ones32, onesb32), (c_ones, ones128), (c_onesrow, onesrow))):
            p.add("gp", lambda s, d_=dst_, s_=src_: s.dma_start(out=d_[:], in_=s_),
                  dma=True, mark=f"ldg{i_}")
        p.add("gp", lambda s: s.dma_start(
            out=lA_sb.ap().rearrange("p (a c) -> p a c", a=NKH),
            in_=lAT.rearrange("(a p) c -> p a c", p=128)), dma=True, mark="ldg6")
        for l in range(NL):
            p.add("gp", lambda s, l=l: s.dma_start(
                out=lB_sb[l][:], in_=lBT[l * RK:(l + 1) * RK, :]), dma=True,
                mark=f"ld_lB{l}")
        LD_GC = [f"ldg{i_}" for i_ in range(7)] +                 [f"ld_lB{l}" for l in range(NL)]
        p.add("dve", lambda s: s.memset(carry_u[:], 0.0), waits=[] + LD_CONSTS)
        p.add("dve", lambda s: s.memset(carry_s[:], 0.0), mark="carry0")

        def emit_prep(src, dst, rows, tag, nch=8):
            rc = rows // nch
            for ch in range(nch):
                p.add("gp", lambda s, ch=ch, src=src, dst=dst, rc=rc: s.dma_start(
                    out=dst[ch * rc:(ch + 1) * rc, :],
                    in_=src[ch * rc:(ch + 1) * rc, :]),
                    waits=[] + LD_GC, dma=True, mark=f"pp_{tag}_{ch}")
            return [f"pp_{tag}_{ch}" for ch in range(nch)]

        prep_wo = emit_prep(WoT, wo_st, DH, "wo")
        prep_w0 = emit_prep(W0T, w0_st, H, "w0")

        for q in range(NTQ):
            bar = sum((p.cur_all(e_) for e_ in ENGINES), [])

            AB.reset()
            AFA.reset()
            xt = AB.alloc(128, NKH * TQ)
            ubuf = AB.alloc(128, TQ)
            gbuf = AB.alloc(128, TQ)
            upad = AFA.alloc(128, TQ + CK - 1)
            cacc = AFA.alloc(128, TQ)

            for sl in range(4):
                p.add("gp", lambda s, sl=sl, q=q: s.dma_start(
                    out=xt.rearrange("p (a t) -> p a t", a=NKH)
                        [:, 8 * sl:8 * sl + 8, :],
                    in_=xT.rearrange("(a p) t -> p a t", p=128)
                        [:, 8 * sl:8 * sl + 8, q * TQ:(q + 1) * TQ]),
                    waits=bar + [] + LD_GC, dma=True, mark=f"xt{q}_{sl}")

            for m in range(2 * NDT):
                bi = m % 2
                waits = [f"xt{q}_{s_}" for s_ in range(4)]
                if m >= 2:
                    waits.append(f"p1mm{q}_{m-2}")
                    if q == 0:
                        waits.append(f"stash{m-2}")
                elif q > 0:
                    waits.append(f"p1mm{q-1}_{2*NDT-2+m}")
                if q == 0:
                    src = WuT if m < NDT else WgT
                    cb = (m % NDT) * 128
                    p.add("gp", lambda s, bi=bi, src=src, cb=cb: s.dma_start(
                        out=wrng[bi].ap().rearrange("p (a c) -> p a c", a=NKH),
                        in_=src.rearrange("(a p) c -> p a c", p=128)
                            [:, :, cb:cb + 128]),
                        waits=waits, dma=True, mark=f"p1w{q}_{m}")
                    p.add("sp", lambda s, bi=bi, m=m: s.dma_start(
                        out=win_st[m], in_=wrng[bi].ap()),
                        waits=[f"p1w{q}_{m}"], dma=True, mark=f"stash{m}")
                else:
                    waits.append(f"stash{m}")
                    p.add("sp", lambda s, bi=bi, m=m: s.dma_start(
                        out=wrng[bi].ap(), in_=win_st[m]),
                        waits=waits, dma=True, mark=f"p1w{q}_{m}")

                for f2 in range(2):
                    bank = 2 * bi + f2
                    for kt in range(NKH):
                        w2 = [f"p1w{q}_{m}"]
                        if kt == 0:
                            if m >= 2:
                                w2.append(f"p1ev{q}_{m-2}_{f2}")
                            elif q > 0:
                                w2.append(f"p1ev{q-1}_{2*NDT-2+m}_{f2}")
                            elif q == 0 and m < 2:
                                w2.append("carry0")
                        p.add("pe", lambda s, bi=bi, kt=kt, f2=f2, bank=bank: s.matmul(
                            psum[bank][:],
                            wrng[bi].ap()[:, kt * 128:(kt + 1) * 128],
                            xt.rearrange("p (a t) -> p a t", a=NKH)
                              [:, kt, f2 * 512:(f2 + 1) * 512],
                            start=(kt == 0), stop=(kt == NKH - 1)),
                            waits=w2,
                            mark=(f"p1mmh{q}_{m}_{f2}" if kt == NKH - 1 else None))
                p.alias(f"p1mm{q}_{m}", f"p1mmh{q}_{m}_1")

                dtl = m % NDT
                if m < NDT:
                    p.add("dve", lambda s, dtl=dtl: s.tensor_copy(
                        out=upad[:, 0:CK - 1],
                        in_=carry_u.ap()[:, dtl * (CK - 1):(dtl + 1) * (CK - 1)]),
                        waits=bar, mark=f"p1ci{q}_{m}")
                    for f2 in range(2):
                        p.add("act", lambda s, f2=f2, bi=bi: s.activation(
                            out=upad[:, CK - 1 + f2 * 512:CK - 1 + (f2 + 1) * 512],
                            in_=psum[2 * bi + f2][:], func=AF.Copy),
                            waits=[f"p1mmh{q}_{m}_{f2}", f"p1ci{q}_{m}"] + bar,
                            mark=f"p1ev{q}_{m}_{f2}")
                    p.add("dve", lambda s, dtl=dtl: s.tensor_copy(
                        out=carry_u.ap()[:, dtl * (CK - 1):(dtl + 1) * (CK - 1)],
                        in_=upad[:, TQ:TQ + CK - 1]),
                        waits=[f"p1ev{q}_{m}_1"], mark=f"p1co{q}_{m}")
                    wcv = [f"p1ev{q}_{m}_0", f"p1ev{q}_{m}_1"]
                    if m > 0:
                        wcv.append(f"p1si{q}_{m-1}")
                    elif q > 0:
                        wcv.append(f"p1si{q-1}_{NDT-1}")
                    p.add("dve", lambda s, dtl=dtl: s.tensor_scalar(
                        out=cacc[:], in0=upad[:, 0:TQ],
                        scalar1=c_convw.ap()[:, dtl * CK:dtl * CK + 1],
                        scalar2=None, op0=Op.mult),
                        waits=wcv)
                    for k in range(1, CK):
                        p.add("dve", lambda s, k=k, dtl=dtl: s.scalar_tensor_tensor(
                            out=cacc[:], in0=upad[:, k:k + TQ],
                            scalar=c_convw.ap()[:, dtl * CK + k:dtl * CK + k + 1],
                            in1=cacc[:], op0=Op.mult, op1=Op.add),
                            mark=(f"p1cv{q}_{m}" if k == CK - 1 else None))
                    wsi = [f"p1cv{q}_{m}"]
                    if m > 0:
                        wsi.append(f"p1us{q}_{dtl-1}")
                    elif q > 0:
                        wsi.append(f"p1us{q-1}_{NDT-1}")
                    p.add("act", lambda s, dtl=dtl: s.activation(
                        out=ubuf[:], in_=cacc[:], func=AF.Silu,
                        bias=col("convb", dtl)),
                        waits=wsi, mark=f"p1si{q}_{m}")
                    p.add("sp", lambda s, q=q, dtl=dtl: s.dma_start(
                        out=u_own[q][dtl * 128:(dtl + 1) * 128, :], in_=ubuf[:]),
                        waits=[f"p1si{q}_{m}"], dma=True, mark=f"p1us{q}_{dtl}")
                    if dtl % 16 == 15:
                        sl = dtl // 16
                        collective(
                            "AllGather", Op.bypass,
                            [u_own[q][sl * 2048:(sl + 1) * 2048, :]], [u_gs[q][sl]],
                            waits=[f"p1us{q}_{d2}" for d2 in
                                   range(sl * 16, (sl + 1) * 16)],
                            mark=f"ug{q}_{sl}")
                        if debug:
                            for r2 in range(2):
                                p.add("gp", lambda s, q=q, sl=sl, r2=r2:
                                      s.dma_start(
                                          out=dbg["u"][r2 * 4096 + sl * 2048:
                                                       r2 * 4096 + (sl + 1) * 2048,
                                                       q * TQ:(q + 1) * TQ],
                                          in_=u_gs[q][sl][r2 * 2048:
                                                          (r2 + 1) * 2048, :]),
                                      waits=[f"ug{q}_{sl}"], dma=True)
                else:
                    for f2 in range(2):
                        wg = [f"p1mmh{q}_{m}_{f2}"] + bar
                        if f2 == 0:
                            if dtl > 0:
                                wg.append(f"p1gs{q}_{dtl-1}")
                            elif q > 0:
                                wg.append(f"p1gs{q-1}_{NDT-1}")
                        p.add("act", lambda s, f2=f2, bi=bi: s.activation(
                            out=gbuf[:, f2 * 512:(f2 + 1) * 512],
                            in_=psum[2 * bi + f2][:], func=AF.Silu),
                            waits=wg, mark=f"p1ev{q}_{m}_{f2}")
                    p.add("sp", lambda s, q=q, dtl=dtl: s.dma_start(
                        out=g_own[dtl * 128:(dtl + 1) * 128, q * TQ:(q + 1) * TQ],
                        in_=gbuf[:]),
                        waits=[f"p1ev{q}_{m}_1"], dma=True, mark=f"p1gs{q}_{dtl}")

            p1_end = sum((p.cur_all(e_) for e_ in ENGINES), [])
            if stage < 2:
                continue

            AB.reset()
            AFA.reset()
            dtn = [AB.alloc(128, TQ) for _ in range(2)]
            bcn = AB.alloc(32, TQ)
            p15_keep = AB.save()
            wx_sb = AB.alloc(128, NKD * SSM)
            usl = [AB.alloc(128, TQ) for _ in range(2)]
            sqs = AB.alloc(128, TQ)
            sqs2 = AB.alloc(128, TQ)
            dt_bf = [AB.alloc(128, TQ) for _ in range(2)]
            bc_bf = AB.alloc(32, TQ)
            rstd_dt = AB.alloc(1, TQ)
            rstd_b = AB.alloc(1, TQ)
            rstd_c = AB.alloc(1, TQ)
            rbc128 = AB.alloc(128, TQ)
            rstd32 = AB.alloc(32, TQ)
            ssm0 = AFA.alloc(128, TQ)
            ssm1 = AFA.alloc(128, TQ)
            ssm2 = AFA.alloc(32, TQ)
            lnt = AFA.alloc(1, TQ)

            p.add("gp", lambda s: s.dma_start(
                out=wx_sb.rearrange("p (a c) -> p a c", a=NKD),
                in_=WxT.rearrange("(a p) c -> p a c", p=128)),
                waits=p1_end, dma=True, mark=f"wx{q}")

            for fp in range(2):
                for kd in range(NKD):
                    g2 = kd // 16
                    sl = g2 % 2
                    row = (kd % 16) * 128 + (g2 // 2) * 2048
                    bi = kd % 2
                    w3 = [f"ug{q}_{sl}"]
                    if kd >= 2:
                        w3.append(f"wxm{q}_{fp}_{kd-2}")
                    elif fp == 1:
                        w3.append(f"wxm{q}_0_{NKD-2+kd}")
                    else:
                        w3 += p1_end
                    p.add("sp", lambda s, sl=sl, row=row, bi=bi, q=q: s.dma_start(
                        out=usl[bi], in_=u_gs[q][sl][row:row + 128, :]),
                        waits=w3, dma=True, mark=f"usl{q}_{fp}_{kd}")
                    for m3 in range(3):
                        mw = 128 if m3 < 2 else 32
                        w4 = [f"usl{q}_{fp}_{kd}", f"wx{q}"]
                        if kd == 0:
                            w4 += p1_end
                        p.add("pe", lambda s, kd=kd, m3=m3, mw=mw, fp=fp, bi=bi:
                              s.matmul(
                                  psum[3 * fp + m3][0:mw, :],
                                  wx_sb.rearrange("p (a c) -> p a c", a=NKD)
                                       [:, kd, m3 * 128:m3 * 128 + mw],
                                  usl[bi][:, fp * 512:(fp + 1) * 512],
                                  start=(kd == 0), stop=(kd == NKD - 1)),
                              waits=w4,
                              mark=(f"wxm{q}_{fp}_{kd}" if m3 == 2 else None))
                for m3, dst, mw in ((0, ssm0, 128), (1, ssm1, 128), (2, ssm2, 32)):
                    p.add("act", lambda s, m3=m3, dst=dst, mw=mw, fp=fp: s.activation(
                        out=dst[0:mw, fp * 512:(fp + 1) * 512],
                        in_=psum[3 * fp + m3][0:mw, :], func=AF.Copy),
                        waits=[f"wxm{q}_{fp}_{NKD-1}"] + p1_end,
                        mark=f"ssme{q}_{fp}_{m3}")

            ssm_done = [f"ssme{q}_{fp}_{m3}" for fp in range(2) for m3 in range(3)]
            if debug:
                for m3, src, mw in ((0, ssm0, 128), (1, ssm1, 128), (2, ssm2, 32)):
                    p.add("sp", lambda s, m3=m3, src=src, mw=mw, q=q: s.dma_start(
                        out=dbg["ssm"][m3 * 128:m3 * 128 + mw, q * TQ:(q + 1) * TQ],
                        in_=src[0:mw, :]), waits=ssm_done, dma=True)

            p.add("act", lambda s: s.activation(
                out=sqs[:], in_=ssm0[:], func=AF.Square),
                waits=ssm_done, mark=f"dsq{q}_0")
            p.add("act", lambda s: s.activation(
                out=sqs2[:], in_=ssm1[:], func=AF.Square),
                waits=ssm_done, mark=f"dsq{q}_1")
            for f2 in range(2):
                w5 = [f"dsq{q}_0", f"dsq{q}_1", ] + LD_GC
                if f2 == 1:
                    w5.append(f"dln{q}_0")
                elif q > 0:
                    w5.append(f"bln{q-1}_1_1")
                p.add("pe", lambda s, f2=f2: s.matmul(
                    psum[6][0:1, :], c_ones.ap()[:, 0:1],
                    sqs[:, f2 * 512:(f2 + 1) * 512],
                    start=True, stop=False), waits=w5)
                p.add("pe", lambda s, f2=f2: s.matmul(
                    psum[6][0:1, :], c_ones.ap()[:, 0:1],
                    sqs2[:, f2 * 512:(f2 + 1) * 512],
                    start=False, stop=True), mark=f"dsqm{q}_{f2}")
                p.add("act", lambda s, f2=f2: s.activation(
                    out=lnt[:, f2 * 512:(f2 + 1) * 512], in_=psum[6][0:1, :],
                    func=AF.Ln, scale=col1("invR"), bias=col1("epsrms")),
                    waits=[f"dsqm{q}_{f2}"], mark=f"dln{q}_{f2}")
            p.add("act", lambda s: s.activation(
                out=rstd_dt[:], in_=lnt[:], func=AF.Exp, scale=col1("neghalf")),
                waits=[f"dln{q}_0", f"dln{q}_1"], mark=f"drs{q}")
            for f2 in range(2):
                w6 = [f"drs{q}", ] + LD_GC
                if f2 == 1:
                    w6.append(f"rbe{q}_0")
                elif q > 0 and stage >= 3:
                    w6.append(f"bbce{q-1}_{2*N-1}_1")
                elif q > 0:
                    w6.append(f"r32e{q-1}_1")
                p.add("pe", lambda s, f2=f2: s.matmul(
                    psum[7][:], c_onesrow.ap()[:],
                    rstd_dt[:, f2 * 512:(f2 + 1) * 512], start=True, stop=True),
                    waits=w6, mark=f"rbm{q}_{f2}")
                p.add("act", lambda s, f2=f2: s.activation(
                    out=rbc128[:, f2 * 512:(f2 + 1) * 512], in_=psum[7][:],
                    func=AF.Copy), waits=[f"rbm{q}_{f2}"], mark=f"rbe{q}_{f2}")
            for i, ss in enumerate((ssm0, ssm1)):
                p.add("act", lambda s, ss=ss, i=i: s.activation(
                    out=dt_bf[i][:], in_=ss[:], func=AF.Copy),
                    waits=ssm_done, mark=f"dbf{q}_{i}")
                p.add("dve", lambda s, i=i: s.tensor_mul(
                    out=dtn[i][:], in0=dt_bf[i][:], in1=rbc128[:]),
                    waits=[f"dbf{q}_{i}", f"rbe{q}_0", f"rbe{q}_1"] + bar,
                    mark=f"dtn{q}_{i}")

            p.add("act", lambda s: s.activation(
                out=sqs[0:32, :], in_=ssm2[:], func=AF.Square),
                waits=ssm_done + [f"dsq{q}_1", f"dsqm{q}_1"], mark=f"bsq{q}")
            blast = f"dln{q}_1"
            for j, rst in ((0, rstd_b), (1, rstd_c)):
                for f2 in range(2):
                    p.add("pe", lambda s, j=j, f2=f2: s.matmul(
                        psum[6][0:1, :], c_ones32.ap()[:, j:j + 1],
                        sqs[0:32, f2 * 512:(f2 + 1) * 512],
                        start=True, stop=True),
                        waits=[f"bsq{q}", blast],
                        mark=f"bsm{q}_{j}_{f2}")
                    blast = f"bln{q}_{j}_{f2}"
                    p.add("act", lambda s, f2=f2: s.activation(
                        out=lnt[:, f2 * 512:(f2 + 1) * 512], in_=psum[6][0:1, :],
                        func=AF.Ln, scale=col1("invN"), bias=col1("epsrms")),
                        waits=[f"bsm{q}_{j}_{f2}"], mark=f"bln{q}_{j}_{f2}")
                p.add("act", lambda s, rst=rst: s.activation(
                    out=rst[:], in_=lnt[:], func=AF.Exp, scale=col1("neghalf")),
                    waits=[f"bln{q}_{j}_0", f"bln{q}_{j}_1"], mark=f"brs{q}_{j}")
            for f2 in range(2):
                p.add("pe", lambda s, f2=f2: s.matmul(
                    psum[7][0:32, :], c_selb.ap()[:],
                    rstd_b[:, f2 * 512:(f2 + 1) * 512], start=True, stop=False),
                    waits=[f"brs{q}_0",
                           f"rbe{q}_1" if f2 == 0 else f"r32e{q}_0"])
                p.add("pe", lambda s, f2=f2: s.matmul(
                    psum[7][0:32, :], c_selc.ap()[:],
                    rstd_c[:, f2 * 512:(f2 + 1) * 512], start=False, stop=True),
                    waits=[f"brs{q}_1"], mark=f"r32m{q}_{f2}")
                p.add("act", lambda s, f2=f2: s.activation(
                    out=rstd32[:, f2 * 512:(f2 + 1) * 512], in_=psum[7][0:32, :],
                    func=AF.Copy), waits=[f"r32m{q}_{f2}"], mark=f"r32e{q}_{f2}")
            p.add("act", lambda s: s.activation(
                out=bc_bf[:], in_=ssm2[:], func=AF.Copy),
                waits=ssm_done, mark=f"bbf{q}")
            p.add("dve", lambda s: s.tensor_mul(
                out=bcn[:], in0=bc_bf[:], in1=rstd32[:]),
                waits=[f"bbf{q}", f"r32e{q}_0", f"r32e{q}_1"], mark=f"bcn{q}")

            p15_end = sum((p.cur_all(e_) for e_ in ENGINES), [])
            if stage < 3:
                continue

            AB.reset(p15_keep)
            AFA.reset()
            bbc = [AB.alloc(128, TQ) for _ in range(2 * N)]
            dAb = [AB.alloc(128, TQ) for _ in range(2)]
            xBb = [AB.alloc(128, TQ) for _ in range(2)]
            stb = [AB.alloc(128, TQ) for _ in range(2)]
            wacc = [AB.alloc(128, TQ) for _ in range(4)]
            wsc = [AB.alloc(128, TQ) for _ in range(2)]
            d_g = [AB.alloc(128, TQ) for _ in range(GRPD)]
            u_g = [AB.alloc(128, TQ) for _ in range(GRPD)]
            du_g = [AB.alloc(128, TQ) for _ in range(GRPD)]
            g_g = [AB.alloc(128, TQ) for _ in range(GRPD)]
            yg_g = [AB.alloc(128, TQ) for _ in range(GRPD)]
            et = AFA.alloc(128, 512)

            for j in range(2 * N):
                for f2 in range(2):
                    w7 = [f"bcn{q}", ] + LD_GC
                    if j == 0 and f2 == 0:
                        w7.append(f"r32e{q}_1")
                    else:
                        w7.append(f"bbce{q}_{j}_{f2-1}" if f2 == 1 else
                                  f"bbce{q}_{j-1}_1")
                    p.add("pe", lambda s, j=j, f2=f2: s.matmul(
                        psum[7][:], c_wstrip.ap()[:, j * 128:(j + 1) * 128],
                        bcn[:, f2 * 512:(f2 + 1) * 512], start=True, stop=True),
                        waits=w7, mark=f"bbcm{q}_{j}_{f2}")
                    w8 = [f"bbcm{q}_{j}_{f2}"]
                    if j == 0 and f2 == 0:
                        w8 += p15_end
                    p.add("act", lambda s, j=j, f2=f2: s.activation(
                        out=bbc[j][:, f2 * 512:(f2 + 1) * 512], in_=psum[7][:],
                        func=AF.Copy), waits=w8, mark=f"bbce{q}_{j}_{f2}")

            for dtl in range(NDT):
                gi = dtl % GRPD
                bi = dtl % 2
                for rk in range(2):
                    w9 = []
                    if dtl >= 2:
                        w9.append(f"dmm{q}_{dtl-2}_1")
                    elif q > 0:
                        w9.append(f"dmm{q-1}_{NDT-2+dtl}_1")
                    else:
                        w9.extend(LD_GC)
                    p.add("gp", lambda s, rk=rk, bi=bi, dtl=dtl: s.dma_start(
                        out=wdtr[bi].ap()[:, rk * 128:(rk + 1) * 128],
                        in_=WdtT[rk * 128:(rk + 1) * 128,
                                 dtl * 128:(dtl + 1) * 128]),
                        waits=w9, dma=True, mark=f"wdl{q}_{dtl}_{rk}")
                    p.add("dve", lambda s, rk=rk, bi=bi: s.tensor_scalar(
                        out=wdtr[bi].ap()[:, rk * 128:(rk + 1) * 128],
                        in0=wdtr[bi].ap()[:, rk * 128:(rk + 1) * 128],
                        scalar1=col("dtlnw", rk), scalar2=None, op0=Op.mult),
                        waits=[f"wdl{q}_{dtl}_{rk}", ] + LD_CONSTS,
                        mark=f"wdf{q}_{dtl}_{rk}")
                for f2 in range(2):
                    for rk in range(2):
                        w10 = [f"wdf{q}_{dtl}_{rk}", f"dtn{q}_{rk}"]
                        if rk == 0:
                            if dtl == 0 and q == 0:
                                w10 += [f"ssme{q}_0_{f2}"]
                            elif dtl == 0:
                                w10 += [f"dex{q-1}_{NDT-1}_{f2}",
                                        f"ssme{q}_0_{f2}"]
                            else:
                                w10.append(f"dex{q}_{dtl-1}_{f2}")
                        p.add("pe", lambda s, f2=f2, rk=rk, dtl=dtl: s.matmul(
                            psum[f2][:],
                            wdtr[dtl % 2].ap()[:, rk * 128:(rk + 1) * 128],
                            dtn[rk][:, f2 * 512:(f2 + 1) * 512],
                            start=(rk == 0), stop=(rk == 1)), waits=w10,
                            mark=(f"dmm{q}_{dtl}_{f2}" if rk == 1 else None))
                    w11 = [f"dmm{q}_{dtl}_{f2}"]
                    if dtl >= GRPD:
                        w11.append(f"dug{q}_{dtl-GRPD}")
                        w11.append(f"dA{q}_{dtl-GRPD}_{N-1}")
                    p.add("act", lambda s, f2=f2, dtl=dtl: s.activation(
                        out=et[:], in_=psum[f2][:], func=AF.Exp,
                        bias=col("dtbias", dtl)),
                        waits=w11, mark=f"dex{q}_{dtl}_{f2}")
                    p.add("act", lambda s, f2=f2, gi=gi: s.activation(
                        out=d_g[gi][:, f2 * 512:(f2 + 1) * 512], in_=et[:],
                        func=AF.Ln, bias=1.0),
                        waits=[f"dex{q}_{dtl}_{f2}"], mark=f"dsf{q}_{dtl}_{f2}")
                if lvl >= 1:
                    p.add("sp", lambda s, dtl=dtl, gi=gi, q=q: s.dma_start(
                        out=dbg["delta"][dtl * 128:(dtl + 1) * 128,
                                         q * TQ:(q + 1) * TQ], in_=d_g[gi][:]),
                        waits=[f"dsf{q}_{dtl}_1"], dma=True)
                w12 = [f"p1us{q}_{dtl}"]
                if dtl >= GRPD:
                    w12.append(f"ydn{q}_{dtl-GRPD}")
                p.add("sp", lambda s, dtl=dtl, gi=gi, q=q: s.dma_start(
                    out=u_g[gi], in_=u_own[q][dtl * 128:(dtl + 1) * 128, :]),
                    waits=w12, dma=True, mark=f"url{q}_{dtl}")
                p.add("dve", lambda s, gi=gi: s.tensor_mul(
                    out=du_g[gi], in0=d_g[gi][:], in1=u_g[gi]),
                    waits=[f"url{q}_{dtl}", f"dsf{q}_{dtl}_1"],
                    mark=f"dug{q}_{dtl}")
                w13 = [f"p1gs{q}_{dtl}"]
                if dtl >= GRPD:
                    w13.append(f"ydn{q}_{dtl-GRPD}")
                p.add("sp", lambda s, dtl=dtl, gi=gi, q=q: s.dma_start(
                    out=g_g[gi],
                    in_=g_own[dtl * 128:(dtl + 1) * 128, q * TQ:(q + 1) * TQ]),
                    waits=w13, dma=True, mark=f"grl{q}_{dtl}")

                for n in range(N):
                    pi = n % 2
                    w14 = [f"dsf{q}_{dtl}_1", ] + LD_CONSTS
                    if n >= 2:
                        w14.append(f"tts{q}_{dtl}_{n-2}")
                    elif dtl > 0:
                        w14.append(f"tts{q}_{dtl-1}_{N-2+n}")
                    elif q > 0:
                        w14.append(f"tts{q-1}_{NDT-1}_{N-2+n}")
                    p.add("act", lambda s, dtl=dtl, n=n, pi=pi, gi=gi: s.activation(
                        out=dAb[pi][:], in_=d_g[gi][:], func=AF.Exp,
                        scale=c_acols.ap()[:, dtl * N + n:dtl * N + n + 1]),
                        waits=w14, mark=f"dA{q}_{dtl}_{n}")
                    p.add("dve", lambda s, n=n, pi=pi, gi=gi: s.tensor_mul(
                        out=xBb[pi][:], in0=du_g[gi], in1=bbc[n][:]),
                        waits=[f"dug{q}_{dtl}", f"bbce{q}_{n}_1"])
                    cidx = dtl * N + n
                    init = 0.0 if q == 0 else carry_s.ap()[:, cidx:cidx + 1]
                    p.add("dve", lambda s, pi=pi, init=init: s.tensor_tensor_scan(
                        out=stb[pi][:], data0=dAb[pi][:], data1=xBb[pi][:],
                        initial=init, op0=Op.mult, op1=Op.add),
                        waits=[f"dA{q}_{dtl}_{n}"], mark=f"tts{q}_{dtl}_{n}")
                    if q < NTQ - 1:
                        p.add("dve", lambda s, pi=pi, cidx=cidx: s.tensor_copy(
                            out=carry_s.ap()[:, cidx:cidx + 1],
                            in_=stb[pi][:, TQ - 1:TQ]))
                    lane = wacc[n % 4]
                    if n < 4:
                        p.add("dve", lambda s, pi=pi, n=n, lane=lane: s.tensor_mul(
                            out=lane[:], in0=stb[pi][:], in1=bbc[N + n][:]),
                            waits=[f"bbce{q}_{N+n}_1"])
                    else:
                        ws_ = wsc[n % 2]
                        p.add("dve", lambda s, pi=pi, n=n, ws_=ws_: s.tensor_mul(
                            out=ws_[:], in0=stb[pi][:], in1=bbc[N + n][:]),
                            waits=[f"bbce{q}_{N+n}_1"])
                        p.add("dve", lambda s, lane=lane, ws_=ws_: s.tensor_add(
                            out=lane[:], in0=lane[:], in1=ws_[:]))
                p.add("dve", lambda s: s.tensor_add(
                    out=wacc[0][:], in0=wacc[0][:], in1=wacc[1][:]))
                p.add("dve", lambda s: s.tensor_add(
                    out=wacc[2][:], in0=wacc[2][:], in1=wacc[3][:]))
                p.add("dve", lambda s: s.tensor_add(
                    out=wacc[0][:], in0=wacc[0][:], in1=wacc[2][:]))
                p.add("dve", lambda s, dtl=dtl, gi=gi: s.scalar_tensor_tensor(
                    out=wacc[0][:], in0=u_g[gi], scalar=col("Dp", dtl),
                    in1=wacc[0][:], op0=Op.mult, op1=Op.add))
                wyg = [f"grl{q}_{dtl}"]
                if dtl >= GRPD:
                    wyg.append(f"ys{q}_{dtl-GRPD}")
                elif q > 0:
                    wyg.append(f"ys{q-1}_{NDT-GRPD+dtl}")
                p.add("dve", lambda s, gi=gi: s.tensor_mul(
                    out=yg_g[gi], in0=wacc[0][:], in1=g_g[gi]),
                    waits=wyg, mark=f"ydn{q}_{dtl}")
                p.add("sp", lambda s, dtl=dtl, gi=gi, q=q: s.dma_start(
                    out=y_own[dtl * 128:(dtl + 1) * 128, q * TQ:(q + 1) * TQ],
                    in_=yg_g[gi]),
                    waits=[f"ydn{q}_{dtl}"], dma=True, mark=f"ys{q}_{dtl}")

        mix_end = sum((p.cur_all(e_) for e_ in ENGINES), [])

        AB.reset()
        AFA.reset()
        y_sb = AB.alloc(128, NDT * FC)
        s_bf = AB.alloc(128, NKH * FC)
        x1my = AB.alloc(128, NMO * FC)
        s2_bf = AB.alloc(128, NMO * FC)
        sq2 = AB.alloc(128, FC)
        attn_ev = AB.alloc(128, FC)
        attn_t = AB.alloc(128, FC)
        mbc = [AB.alloc(128, FC) for _ in range(2)]
        h_sb = [AB.alloc(RK, FC) for _ in range(NL)]
        sel_t = AB.alloc(128, FC)
        msk_t = AB.alloc(128, FC)
        ylv = [AB.alloc(128, FC) for _ in range(NL)]
        x0_t = AFA.alloc(128, FC)
        s_t = AFA.alloc(128, FC)
        base_t = AFA.alloc(128, FC)
        mrow = AFA.alloc(1, 4 * FC)
        st2a = AFA.alloc(1, FC)
        st2b = AFA.alloc(1, FC)
        x2t = AFA.alloc(128, FC)
        mrbf = AB.alloc(1, 2 * FC)
        xst = AFA.alloc(128, NMO * 128 // 4)

        def stageA(fc):
            par = fc % 2
            w = mix_end + [f"ys{NTQ-1}_{NDT-1}"]
            if fc > 0:
                w.append(f"wom{fc-1}_{NKH-1}")
            p.add("sp", lambda s, fc=fc: s.dma_start(
                out=y_sb.rearrange("p (a t) -> p a t", a=NDT),
                in_=y_own.rearrange("(a p) t -> p a t", p=128)
                    [:, :, fc * FC:(fc + 1) * FC]),
                waits=w, dma=True, mark=f"yld{fc}")
            for m in range(NKH):
                bi = m % 2
                wl = list(prep_wo)
                if m >= 2:
                    wl.append(f"wom{fc}_{m-2}")
                elif fc > 0:
                    wl.append(f"w0m{fc-1}_{NMO-2+m}")
                else:
                    wl += mix_end
                p.add("sp", lambda s, m=m, bi=bi: s.dma_start(
                    out=wrng[bi].ap().rearrange("p (a c) -> p a c", a=NDT),
                    in_=wo_st.rearrange("(a p) c -> p a c", p=128)
                        [:, :, m * 128:(m + 1) * 128]),
                    waits=wl, dma=True, mark=f"wol{fc}_{m}")
                for kt in range(NDT):
                    wm = [f"wol{fc}_{m}", f"yld{fc}"]
                    if kt == 0:
                        if fc == 0 and m < 2:
                            wm += mix_end
                        else:
                            wm.append(f"woev{fc}_{m-2}" if m >= 2 else
                                      f"bse{fc-1}_{NMO-2+m}")
                    p.add("pe", lambda s, m=m, kt=kt, bi=bi: s.matmul(
                        psum[bi][:],
                        wrng[bi].ap()[:, kt * 128:(kt + 1) * 128],
                        y_sb.rearrange("p (a t) -> p a t", a=NDT)[:, kt, :],
                        start=(kt == 0), stop=(kt == NDT - 1)),
                        waits=wm,
                        mark=(f"wom{fc}_{m}" if kt == NDT - 1 else None))
                we = [f"wom{fc}_{m}"]
                if m > 0:
                    we.append(f"atw{fc}_{m-1}")
                elif fc > 0:
                    we.append(f"atw{fc-1}_{NKH-1}")
                else:
                    we += mix_end
                p.add("act", lambda s, bi=bi: s.activation(
                    out=attn_ev[:], in_=psum[bi][:], func=AF.Copy),
                    waits=we, mark=f"woev{fc}_{m}")
                ws_ = [f"woev{fc}_{m}"]
                if fc >= 2:
                    ws_.append(f"rs{fc-2}")
                p.add("sp", lambda s, m=m, par=par: s.dma_start(
                    out=attn_in[par][m * 128:(m + 1) * 128, :], in_=attn_ev[:]),
                    waits=ws_, dma=True, mark=f"atw{fc}_{m}")
            wrs = [f"atw{fc}_{m}" for m in range(NKH)]
            wag = [f"rs{fc}"]
            if fc >= 2:
                wrs.append(f"aml{fc-2}_{NMO-1}")
                wag.append(f"atl{fc-2}_{NKH-1}")
            collective("ReduceScatter", Op.add, [attn_in[par]], [attn_rs[par]],
                       waits=wrs, mark=f"rs{fc}")
            collective("AllGather", Op.bypass, [attn_rs[par]], [attn_ag[par]],
                       waits=wag, mark=f"ag{fc}")

        def stageB(fc):
            par = fc % 2
            for kt in range(NKH):
                wx0 = []
                if kt > 0:
                    wx0.append(f"sad{fc}_{kt-1}")
                elif fc > 0:
                    wx0 += [f"sad{fc-1}_{NKH-1}", f"x1my{fc-1}_{NMO-1}"]
                else:
                    wx0 += mix_end
                p.add("sp", lambda s, kt=kt, fc=fc: s.dma_start(
                    out=x0_t, in_=xT[kt * 128:(kt + 1) * 128,
                                     fc * FC:(fc + 1) * FC]),
                    waits=wx0, dma=True, mark=f"x0l{fc}_{kt}")
                p.add("gp", lambda s, kt=kt, par=par: s.dma_start(
                    out=attn_t, in_=attn_ag[par][kt * 128:(kt + 1) * 128, :]),
                    waits=[f"ag{fc}"] + wx0, dma=True, mark=f"atl{fc}_{kt}")
                wad = [f"x0l{fc}_{kt}", f"atl{fc}_{kt}"]
                if kt > 0:
                    wad += [f"ssq{fc}_{kt-1}", f"lnsm{fc}_{kt-1}"]
                elif fc > 0:
                    wad += [f"ssq{fc-1}_{NKH-1}", f"lnsm{fc-1}_{NKH-1}"]
                else:
                    wad += mix_end
                p.add("dve", lambda s: s.tensor_add(
                    out=s_t, in0=x0_t, in1=attn_t),
                    waits=wad, mark=f"sad{fc}_{kt}")
                p.add("act", lambda s, kt=kt: s.activation(
                    out=s_bf[:, kt * FC:(kt + 1) * FC], in_=s_t, func=AF.Copy),
                    waits=[f"sad{fc}_{kt}"] +
                          (mix_end if fc == 0 and kt == 0 else []) +
                          ([f"w0m{fc-1}_{NMO-1}"] if fc > 0 and kt == 0 else []),
                    mark=f"sbf{fc}_{kt}")
                p.add("act", lambda s: s.activation(
                    out=sq2[:], in_=s_t, func=AF.Square),
                    waits=[f"sad{fc}_{kt}"] +
                          ([f"lnsm{fc}_{kt-1}"] if kt > 0 else
                           ([f"lns2{fc-1}_{NMO-1}"] if fc > 0 else [])),
                    mark=f"ssq{fc}_{kt}")
                wst = [f"sbf{fc}_{kt}", f"ssq{fc}_{kt}"]
                if kt == 0:
                    if fc > 0:
                        wst += [f"ylm{fc-1}_{NMO-1}_0", f"ylm{fc-1}_{NMO-1}_1",
                                f"var{fc-1}"]
                    else:
                        wst += mix_end
                p.add("pe", lambda s, kt=kt: s.matmul(
                    psum[2][0:1, :], c_ones.ap()[:, 0:1],
                    s_bf[:, kt * FC:(kt + 1) * FC],
                    start=(kt == 0), stop=(kt == NKH - 1)), waits=wst)
                p.add("pe", lambda s, kt=kt: s.matmul(
                    psum[3][0:1, :], c_ones.ap()[:, 0:1], sq2[:],
                    start=(kt == 0), stop=(kt == NKH - 1)),
                    mark=f"lnsm{fc}_{kt}")
            p.alias(f"lnm{fc}", f"lnsm{fc}_{NKH-1}")

            p.add("dve", lambda s: s.tensor_scalar(
                out=mrow[:, 0:FC], in0=psum[2][0:1, :], scalar1=1.0 / H,
                scalar2=None, op0=Op.mult), waits=[f"lnm{fc}"], mark=f"mu{fc}")
            p.add("dve", lambda s: s.tensor_scalar(
                out=mrow[:, 3 * FC:4 * FC], in0=psum[3][0:1, :], scalar1=1.0 / H,
                scalar2=None, op0=Op.mult))
            p.add("dve", lambda s: s.tensor_mul(
                out=mrow[:, 2 * FC:3 * FC], in0=mrow[:, 0:FC], in1=mrow[:, 0:FC]))
            p.add("dve", lambda s: s.tensor_sub(
                out=mrow[:, 3 * FC:4 * FC], in0=mrow[:, 3 * FC:4 * FC],
                in1=mrow[:, 2 * FC:3 * FC]), mark=f"var{fc}")
            p.add("act", lambda s: s.activation(
                out=mrow[:, 2 * FC:3 * FC], in_=mrow[:, 3 * FC:4 * FC],
                func=AF.Ln, bias=col1("epsln")), waits=[f"var{fc}"], mark=f"lnv{fc}")
            p.add("act", lambda s: s.activation(
                out=mrow[:, FC:2 * FC], in_=mrow[:, 2 * FC:3 * FC],
                func=AF.Exp, scale=col1("neghalf")), waits=[f"lnv{fc}"], mark=f"rst{fc}")
            p.add("dve", lambda s: s.tensor_mul(
                out=mrow[:, 2 * FC:3 * FC], in0=mrow[:, 0:FC],
                in1=mrow[:, FC:2 * FC]),
                waits=[f"rst{fc}", f"mu{fc}"], mark=f"mrs{fc}")
            p.add("act", lambda s: s.activation(
                out=mrbf[:, 0:FC], in_=mrow[:, FC:2 * FC], func=AF.Copy),
                waits=[f"mrs{fc}"], mark=f"mrb{fc}_0")
            p.add("act", lambda s: s.activation(
                out=mrbf[:, FC:2 * FC], in_=mrow[:, 2 * FC:3 * FC],
                func=AF.Copy), waits=[f"mrs{fc}"], mark=f"mrb{fc}_1")
            for j, off in ((0, 0), (1, FC)):
                wb = [f"mrb{fc}_{j}"]
                if j == 1:
                    wb.append(f"lbe{fc}_0")
                elif fc > 0:
                    wb.append(f"lb2e{fc-1}_1")
                p.add("pe", lambda s, off=off: s.matmul(
                    psum[4][:], c_onesrow.ap()[:], mrbf[:, off:off + FC],
                    start=True, stop=True), waits=wb, mark=f"lbm{fc}_{j}")
                wbe = [f"lbm{fc}_{j}"]
                if fc > 0 and j == 0:
                    wbe.append(f"x2{fc-1}_{FC//128-1}_{NMO-1}")
                p.add("act", lambda s, j=j: s.activation(
                    out=mbc[j][:], in_=psum[4][:], func=AF.Copy),
                    waits=wbe, mark=f"lbe{fc}_{j}")
            for kt in range(NKH):
                p.add("dve", lambda s, kt=kt: s.tensor_mul(
                    out=s_bf[:, kt * FC:(kt + 1) * FC],
                    in0=s_bf[:, kt * FC:(kt + 1) * FC], in1=mbc[0][:]),
                    waits=[f"lbe{fc}_0", f"sbf{fc}_{kt}"])
                p.add("dve", lambda s, kt=kt: s.tensor_sub(
                    out=s_bf[:, kt * FC:(kt + 1) * FC],
                    in0=s_bf[:, kt * FC:(kt + 1) * FC], in1=mbc[1][:]),
                    waits=[f"lbe{fc}_1"], mark=f"x1{fc}_{kt}")
            for j in range(NMO):
                wj = []
                if j > 0:
                    wj.append(f"x1my{fc}_{j-1}")
                elif fc > 0:
                    wj.append(f"x1my{fc-1}_{NMO-1}")
                else:
                    wj += mix_end
                wj.append(f"sad{fc}_{NKH-1}")
                p.add("sp", lambda s, j=j, fc=fc: s.dma_start(
                    out=x0_t, in_=x0my[j * 128:(j + 1) * 128,
                                       fc * FC:(fc + 1) * FC]),
                    waits=wj, dma=True, mark=f"xml{fc}_{j}")
                p.add("gp", lambda s, j=j, par=par: s.dma_start(
                    out=attn_t, in_=attn_rs[par][j * 128:(j + 1) * 128, :]),
                    waits=[f"ag{fc}"] + wj, dma=True, mark=f"aml{fc}_{j}")
                p.add("dve", lambda s: s.tensor_add(
                    out=s_t, in0=x0_t, in1=attn_t),
                    waits=[f"xml{fc}_{j}", f"aml{fc}_{j}", f"ssq{fc}_{NKH-1}",
                           f"lnsm{fc}_{NKH-1}"])
                p.add("dve", lambda s, j=j: s.tensor_mul(
                    out=x1my[:, j * FC:(j + 1) * FC], in0=s_t, in1=mbc[0][:]),
                    waits=[f"lbe{fc}_0"])
                p.add("dve", lambda s, j=j: s.tensor_sub(
                    out=x1my[:, j * FC:(j + 1) * FC],
                    in0=x1my[:, j * FC:(j + 1) * FC], in1=mbc[1][:]),
                    waits=[f"lbe{fc}_1"], mark=f"x1my{fc}_{j}")

            if debug:
                for kt in range(NKH):
                    p.add("sp", lambda s, kt=kt, fc=fc: s.dma_start(
                        out=dbg["x1"][kt * 128:(kt + 1) * 128,
                                      fc * FC:(fc + 1) * FC],
                        in_=s_bf[:, kt * FC:(kt + 1) * FC]),
                        waits=[f"x1{fc}_{kt}"], dma=True)
                    p.add("gp", lambda s, kt=kt, par=par: s.dma_start(
                        out=dbg["attn"][kt * 128:(kt + 1) * 128,
                                        fc * FC:(fc + 1) * FC],
                        in_=attn_ag[par][kt * 128:(kt + 1) * 128, :]),
                        waits=[f"ag{fc}"], dma=True)

            for l in range(NL):
                for kt in range(NKH):
                    wh = [f"x1{fc}_{kt}"]
                    if kt == 0:
                        if l > 0:
                            wh.append(f"hev{fc}_{l-1}")
                        elif fc > 0:
                            wh.append(f"xse{fc-1}_{FC//128-1}_{NMO-1}")
                    p.add("pe", lambda s, l=l, kt=kt: s.matmul(
                        psum[5][0:RK, :],
                        lA_sb.ap()[:, kt * NL * RK + l * RK:
                                   kt * NL * RK + (l + 1) * RK],
                        s_bf[:, kt * FC:(kt + 1) * FC],
                        start=(kt == 0), stop=(kt == NKH - 1)), waits=wh,
                        mark=(f"hmm{fc}_{l}" if kt == NKH - 1 else None))
                p.add("act", lambda s, l=l: s.activation(
                    out=h_sb[l][:], in_=psum[5][0:RK, :], func=AF.Copy),
                    waits=[f"hmm{fc}_{l}"], mark=f"hev{fc}_{l}")
            p.add("sp", lambda s, fc=fc: s.dma_start(
                out=lid_sb.ap().rearrange("p (a t) -> p a t", a=NMO),
                in_=lidx.rearrange("(a p) t -> p a t", p=128)
                    [:, :, fc * FC:(fc + 1) * FC]),
                waits=[f"sel{fc-1}_{NMO-1}"] if fc > 0 else mix_end,
                dma=True, mark=f"lidl{fc}")

            for m in range(NMO):
                bi = m % 2
                ww = list(prep_w0)
                ww.append(f"w0m{fc}_{m-2}" if m >= 2 else
                          f"wom{fc}_{NKH-2+m}")
                p.add("sp", lambda s, m=m, bi=bi: s.dma_start(
                    out=wrng[bi].ap().rearrange("p (a c) -> p a c", a=NKH),
                    in_=w0_st.rearrange("(a p) c -> p a c", p=128)
                        [:, :, m * 128:(m + 1) * 128]),
                    waits=ww, dma=True, mark=f"w0l{fc}_{m}")
                for kt in range(NKH):
                    wk = [f"w0l{fc}_{m}", f"x1{fc}_{kt}"]
                    if kt == 0:
                        wk.append(f"bse{fc}_{m-2}" if m >= 2 else
                                  f"woev{fc}_{NKH-2+m}")
                    p.add("pe", lambda s, m=m, kt=kt, bi=bi: s.matmul(
                        psum[bi][:],
                        wrng[bi].ap()[:, kt * 128:(kt + 1) * 128],
                        s_bf[:, kt * FC:(kt + 1) * FC],
                        start=(kt == 0), stop=(kt == NKH - 1)), waits=wk,
                        mark=(f"w0m{fc}_{m}" if kt == NKH - 1 else None))
                p.add("act", lambda s, m=m, bi=bi: s.activation(
                    out=base_t, in_=psum[bi][:], func=AF.Identity,
                    bias=col("b0c", m)),
                    waits=[f"w0m{fc}_{m}"] +
                          ([f"s2f{fc}_{m-1}"] if m > 0 else
                           ([f"s2f{fc-1}_{NMO-1}"] if fc > 0 else [])),
                    mark=f"bse{fc}_{m}")
                for l in range(NL):
                    bank = 2 + l
                    wy = [f"hev{fc}_{l}", f"ld_lB{l}"]
                    if m > 0:
                        wy.append(f"yle{fc}_{m-1}_{l}")
                    elif l in (0, 1):
                        wy.append(f"var{fc}")
                    elif fc > 0:
                        wy.append(f"lb2e{fc-1}_1" if l == 2 else
                                  f"xse{fc-1}_{FC//128-1}_{NMO-1}")
                    p.add("pe", lambda s, m=m, l=l, bank=bank: s.matmul(
                        psum[bank][:], lB_sb[l].ap()[:, m * 128:(m + 1) * 128],
                        h_sb[l][:], start=True, stop=True),
                        waits=wy, mark=f"ylm{fc}_{m}_{l}")
                    p.add("act", lambda s, l=l, bank=bank: s.activation(
                        out=ylv[l][:], in_=psum[bank][:], func=AF.Copy),
                        waits=[f"ylm{fc}_{m}_{l}"], mark=f"yle{fc}_{m}_{l}")
                for l in range(NL):
                    p.add("dve", lambda s, m=m, l=l: s.tensor_scalar(
                        out=msk_t[:], in0=lid_sb.ap()[:, m * FC:(m + 1) * FC],
                        scalar1=float(l), scalar2=8.0,
                        op0=Op.is_equal, op1=Op.mult),
                        waits=[f"lidl{fc}"])
                    if l == 0:
                        p.add("dve", lambda s, l=l: s.tensor_mul(
                            out=sel_t[:], in0=msk_t[:], in1=ylv[l][:]),
                            waits=[f"yle{fc}_{m}_{l}"])
                    else:
                        p.add("dve", lambda s, l=l: s.tensor_mul(
                            out=msk_t[:], in0=msk_t[:], in1=ylv[l][:]),
                            waits=[f"yle{fc}_{m}_{l}"])
                        p.add("dve", lambda s: s.tensor_add(
                            out=sel_t[:], in0=sel_t[:], in1=msk_t[:]),
                            mark=(f"sel{fc}_{m}" if l == NL - 1 else None))
                ws2 = [f"sel{fc}_{m}", f"bse{fc}_{m}"]
                if m > 0:
                    ws2 += [f"s2q{fc}_{m-1}", f"lns2{fc}_{m-1}"]
                else:
                    ws2 += [f"ssq{fc}_{NKH-1}", f"lnsm{fc}_{NKH-1}"]
                p.add("dve", lambda s, m=m: s.tensor_add(
                    out=s_t, in0=base_t, in1=sel_t[:]), waits=ws2)
                p.add("dve", lambda s, m=m: s.tensor_add(
                    out=s_t, in0=s_t, in1=x1my[:, m * FC:(m + 1) * FC]),
                    waits=[f"x1my{fc}_{m}"], mark=f"s2{fc}_{m}")
                p.add("act", lambda s, m=m: s.activation(
                    out=s2_bf[:, m * FC:(m + 1) * FC], in_=s_t, func=AF.Copy),
                    waits=[f"s2{fc}_{m}"], mark=f"s2b{fc}_{m}")
                p.add("act", lambda s, m=m: s.activation(
                    out=sq2[:], in_=s_t, func=AF.Square),
                    waits=[f"s2{fc}_{m}"] +
                          ([f"lns2{fc}_{m-1}"] if m > 0 else
                           [f"lnsm{fc}_{NKH-1}"]),
                    mark=f"s2q{fc}_{m}")
                p.alias(f"s2f{fc}_{m}", f"s2{fc}_{m}")
                wq = [f"s2b{fc}_{m}", f"s2q{fc}_{m}"]
                if m == 0:
                    wq.append(f"st2c{fc-1}" if fc > 0 else f"lbe{fc}_1")
                p.add("pe", lambda s, m=m: s.matmul(
                    psum[6][0:1, :], c_ones.ap()[:, 0:1],
                    s2_bf[:, m * FC:(m + 1) * FC],
                    start=(m == 0), stop=(m == NMO - 1)), waits=wq)
                p.add("pe", lambda s, m=m: s.matmul(
                    psum[7][0:1, :], c_ones.ap()[:, 0:1], sq2[:],
                    start=(m == 0), stop=(m == NMO - 1)),
                    mark=f"lns2{fc}_{m}")
            p.alias(f"st2m{fc}", f"lns2{fc}_{NMO-1}")
            p.add("dve", lambda s: s.tensor_copy(out=st2a,
                                                 in_=psum[6][0:1, :]),
                  waits=[f"st2m{fc}"])
            p.add("dve", lambda s: s.tensor_copy(out=st2b,
                                                 in_=psum[7][0:1, :]),
                  mark=f"st2c{fc}")
            wsw = [f"st2c{fc}"]
            if fc >= 2:
                wsw.append(f"ccst{fc-2}")
            p.add("sp", lambda s, par=par: s.dma_start(
                out=stats_in[par][0:1, :], in_=st2a), waits=wsw, dma=True,
                mark=f"stwa{fc}")
            p.add("sp", lambda s, par=par: s.dma_start(
                out=stats_in[par][1:2, :], in_=st2b), waits=wsw, dma=True,
                mark=f"stw{fc}")
            collective("AllReduce", Op.add, [stats_in[par]], [stats_out[par]],
                       waits=[f"stw{fc}", f"stwa{fc}"], mark=f"ccst{fc}")
            p.add("sp", lambda s, par=par: s.dma_start(
                out=st2a, in_=stats_out[par][0:1, :]),
                waits=[f"ccst{fc}"], dma=True, mark=f"stra{fc}")
            p.add("sp", lambda s, par=par: s.dma_start(
                out=st2b, in_=stats_out[par][1:2, :]),
                waits=[f"ccst{fc}"], dma=True, mark=f"str{fc}")
            p.add("dve", lambda s: s.tensor_scalar(
                out=mrow[:, 0:FC], in0=st2a, scalar1=1.0 / H,
                scalar2=None, op0=Op.mult), waits=[f"str{fc}", f"stra{fc}"], mark=f"mu2{fc}")
            p.add("dve", lambda s: s.tensor_scalar(
                out=mrow[:, 3 * FC:4 * FC], in0=st2b, scalar1=1.0 / H,
                scalar2=None, op0=Op.mult))
            p.add("dve", lambda s: s.tensor_mul(
                out=mrow[:, 2 * FC:3 * FC], in0=mrow[:, 0:FC], in1=mrow[:, 0:FC]))
            p.add("dve", lambda s: s.tensor_sub(
                out=mrow[:, 3 * FC:4 * FC], in0=mrow[:, 3 * FC:4 * FC],
                in1=mrow[:, 2 * FC:3 * FC]), mark=f"var2{fc}")
            p.add("act", lambda s: s.activation(
                out=mrow[:, 2 * FC:3 * FC], in_=mrow[:, 3 * FC:4 * FC],
                func=AF.Ln, bias=col1("epsln")), waits=[f"var2{fc}"], mark=f"lnv2{fc}")
            p.add("act", lambda s: s.activation(
                out=mrow[:, FC:2 * FC], in_=mrow[:, 2 * FC:3 * FC],
                func=AF.Exp, scale=col1("neghalf")), waits=[f"lnv2{fc}"], mark=f"rs2{fc}")
            p.add("dve", lambda s: s.tensor_mul(
                out=mrow[:, 2 * FC:3 * FC], in0=mrow[:, 0:FC],
                in1=mrow[:, FC:2 * FC]),
                waits=[f"rs2{fc}", f"mu2{fc}"], mark=f"mrs2{fc}")
            p.add("act", lambda s: s.activation(
                out=mrbf[:, 0:FC], in_=mrow[:, FC:2 * FC], func=AF.Copy),
                waits=[f"mrs2{fc}"], mark=f"mr2b{fc}_0")
            p.add("act", lambda s: s.activation(
                out=mrbf[:, FC:2 * FC], in_=mrow[:, 2 * FC:3 * FC],
                func=AF.Copy), waits=[f"mrs2{fc}"], mark=f"mr2b{fc}_1")
            for j, off in ((0, 0), (1, FC)):
                wb2 = [f"mr2b{fc}_{j}",
                       f"lbe{fc}_1" if j == 0 else f"lb2e{fc}_0"]
                p.add("pe", lambda s, off=off: s.matmul(
                    psum[4][:], c_onesrow.ap()[:], mrbf[:, off:off + FC],
                    start=True, stop=True), waits=wb2, mark=f"lb2m{fc}_{j}")
                p.add("act", lambda s, j=j: s.activation(
                    out=mbc[j][:], in_=psum[4][:], func=AF.Copy),
                    waits=[f"lb2m{fc}_{j}"], mark=f"lb2e{fc}_{j}")
            for ts4 in range(FC // 128):
                for m in range(NMO):
                    wx2 = [f"lb2e{fc}_0", f"s2b{fc}_{m}"]
                    if (ts4, m) != (0, 0):
                        pm = (ts4, m - 1) if m > 0 else (ts4 - 1, NMO - 1)
                        wx2.append(f"xtr{fc}_{pm[0]}_{pm[1]}")
                    elif fc > 0:
                        wx2.append(f"xtr{fc-1}_{FC//128-1}_{NMO-1}")
                    p.add("dve", lambda s, m=m: s.tensor_mul(
                        out=x2t, in0=s2_bf[:, m * FC:(m + 1) * FC],
                        in1=mbc[0][:]), waits=wx2)
                    p.add("dve", lambda s, m=m: s.tensor_sub(
                        out=x2t, in0=x2t, in1=mbc[1][:]),
                        waits=[f"lb2e{fc}_1"], mark=f"x2{fc}_{ts4}_{m}")
                    wt = [f"x2{fc}_{ts4}_{m}", f"hev{fc}_{NL-1}"] + LD_CONSTS
                    if (ts4, m) != (0, 0):
                        pm = (ts4, m - 1) if m > 0 else (ts4 - 1, NMO - 1)
                        wt.append(f"xse{fc}_{pm[0]}_{pm[1]}")
                    else:
                        wt.append(f"yle{fc}_{NMO-1}_{NL-1}")
                        if fc > 0:
                            wt.append(f"xout{fc-1}_{FC//128-1}")
                    p.add("pe", lambda s, ts4=ts4: s.transpose(
                        psum[5][:, 0:128], x2t[:, ts4 * 128:(ts4 + 1) * 128],
                        c_ident[:]), waits=wt, mark=f"xtr{fc}_{ts4}_{m}")
                    p.add("act", lambda s, m=m: s.activation(
                        out=xst[:, (m % 4) * 128:(m % 4 + 1) * 128],
                        in_=psum[5][:, 0:128], func=AF.Copy),
                        waits=[f"xtr{fc}_{ts4}_{m}"] +
                              ([f"xo{fc}_{ts4}_{(m // 4) * 4 - 1}"] if m >= 4
                               else ([f"xo{fc}_{ts4-1}_{NMO-1}"] if ts4 > 0
                                     else ([f"xo{fc-1}_{FC//128-1}_{NMO-1}"]
                                           if fc > 0 else []))),
                        mark=f"xse{fc}_{ts4}_{m}")
                    if m % 4 == 3:
                        p.add("sp", lambda s, fc=fc, ts4=ts4, m=m: s.dma_start(
                            out=out_half[fc * FC + ts4 * 128:
                                         fc * FC + (ts4 + 1) * 128,
                                         (m - 3) * 128:(m + 1) * 128],
                            in_=xst[:, 0:512]),
                            waits=[f"xse{fc}_{ts4}_{mm}"
                                   for mm in range(m - 3, m + 1)],
                            dma=True, mark=f"xo{fc}_{ts4}_{m}")
                p.alias(f"xout{fc}_{ts4}", f"xo{fc}_{ts4}_{NMO-1}")

        if stage >= 4:
            for fc in range(NFC):
                stageA(fc)
                stageB(fc)

        if stage >= 4:
            fin = ([f"xo{fc}_{t4}_{NMO-1}" for fc in range(NFC)
                    for t4 in range(FC // 128)] +
                   [f"ccst{NFC-1}", f"str{NFC-1}", f"stra{NFC-1}"])
        elif stage >= 3:
            fin = [f"ys{q}_{d}" for q in range(NTQ) for d in range(NDT)]
        elif stage >= 2:
            fin = ([f"ssme{q}_1_2" for q in range(NTQ)] +
                   [f"bcn{q}" for q in range(NTQ)] +
                   [f"p1gs{q}_{NDT-1}" for q in range(NTQ)])
        else:
            fin = ([f"p1us{q}_{d}" for q in range(NTQ) for d in range(NDT)] +
                   [f"p1gs{q}_{NDT-1}" for q in range(NTQ)] +
                   [f"ug{q}_{sl}" for q in range(NTQ) for sl in range(2)])
        if lvl == 3 and stage >= 4:
            p.add("sp", lambda s: s.dma_start(
                out=dbg["arena_bf"], in_=AB.h.ap()), waits=fin, dma=True,
                mark="dump_ab")
            p.add("sp", lambda s: s.dma_start(
                out=dbg["arena_f32"], in_=AFA.h.ap()), waits=fin, dma=True,
                mark="dump_af")
            fin = fin + ["dump_ab", "dump_af"]
        p.add("sp", lambda s: s.nop(), waits=fin, sem=None)

        p.emit(block, sems)
    return nc, (AB.hwm, AFA.hwm)


def kernel(**inp):
    f32 = np.float32
    x = np.asarray(inp["L0_attn_hidden_states"], f32)
    Win = np.asarray(inp["L0_attn_W_in"], f32)
    conv_w = np.asarray(inp["L0_attn_conv_w"], f32)
    conv_b = np.asarray(inp["L0_attn_conv_b"], f32)
    Wx = np.asarray(inp["L0_attn_Wx"], f32)
    dt_ln_w = np.asarray(inp["L0_attn_dt_ln_w"], f32)
    b_ln_w = np.asarray(inp["L0_attn_b_ln_w"], f32)
    c_ln_w = np.asarray(inp["L0_attn_c_ln_w"], f32)
    Wdt = np.asarray(inp["L0_attn_Wdt"], f32)
    dt_bias = np.asarray(inp["L0_attn_dt_bias"], f32)
    A = np.asarray(inp["L0_attn_A"], f32)
    D_param = np.asarray(inp["L0_attn_D_param"], f32)
    W_out = np.asarray(inp["L0_attn_W_out"], f32)
    W0 = np.asarray(inp["L0_ffn_W0"], f32)
    b0 = np.asarray(inp["L0_ffn_b0"], f32)
    lA = np.asarray(inp["L0_ffn_A"], f32)
    lB = np.asarray(inp["L0_ffn_B"], f32)

    lidx_full = _lora_index_table()
    ident = np.eye(128, dtype=f32)
    ones128 = np.ones((128, 2), f32)
    onesrow = np.ones((1, 128), f32)
    onesb32 = np.zeros((32, 2), f32)
    onesb32[0:16, 0] = 1.0
    onesb32[16:32, 1] = 1.0
    selb = np.zeros((1, 32), f32)
    selb[0, 0:16] = 1.0
    selc = np.zeros((1, 32), f32)
    selc[0, 16:32] = 1.0
    gains = np.concatenate([b_ln_w, c_ln_w])
    wstrip = np.zeros((32, 32 * 128), f32)
    for j in range(32):
        wstrip[j, j * 128:(j + 1) * 128] = gains[j]

    key = (DEBUG, STAGE, NOCC, PESTRICT)
    if key not in _NC_CACHE:
        _NC_CACHE[key] = build_nc(debug=DEBUG, stage=STAGE, nocc=NOCC,
                                  pestrict=PESTRICT)
    nc, _ = _NC_CACHE[key]

    def packcols(v):
        return np.ascontiguousarray(v.reshape(-1, 128).T)

    in_maps = []
    for c in range(8):
        b, h = c >> 1, c & 1
        dlo = h * DH
        olo = h * OH
        cpack = np.concatenate([
            packcols(conv_b[dlo:dlo + DH]),
            packcols(dt_bias[dlo:dlo + DH]),
            packcols(D_param[dlo:dlo + DH]),
            packcols(dt_ln_w),
            packcols(b0[olo:olo + OH]),
            np.tile(np.array([[1.0 / R, 1.0 / N, -0.5, EPS_RMS, EPS_LN]],
                             dtype=f32), (128, 1)),
        ], axis=1)
        in_maps.append({
            "xT": np.ascontiguousarray(x[b].T),
            "x0my": np.ascontiguousarray(x[b].T[olo:olo + OH]),
            "WuT": np.ascontiguousarray(Win[dlo:dlo + DH].T),
            "WgT": np.ascontiguousarray(Win[D + dlo:D + dlo + DH].T),
            "WxT": np.ascontiguousarray(Wx.T),
            "WdtT": np.ascontiguousarray(Wdt[dlo:dlo + DH].T),
            "WoT": np.ascontiguousarray(W_out[:, dlo:dlo + DH].T),
            "W0T": np.ascontiguousarray(W0[olo:olo + OH].T),
            "lAT": np.ascontiguousarray(lA.reshape(NL * RK, H).T),
            "lBT": np.ascontiguousarray(
                np.concatenate([lB[l, olo:olo + OH].T for l in range(NL)],
                               axis=0)),
            "A_cols": np.ascontiguousarray(
                A[dlo:dlo + DH].reshape(NDT, 128, N).transpose(1, 0, 2)
                .reshape(128, NDT * N)),
            "convw": np.ascontiguousarray(
                conv_w[dlo:dlo + DH].reshape(NDT, 128, CK).transpose(1, 0, 2)
                .reshape(128, NDT * CK)),
            "cpack": np.ascontiguousarray(cpack),
            "wstrip": wstrip,
            "selb": selb,
            "selc": selc,
            "onesb32": onesb32,
            "ones128": ones128,
            "onesrow": onesrow,
            "ident": ident,
            "lidx": np.ascontiguousarray(lidx_full[b][:, olo:olo + OH].T),
        })

    res = run_bass_kernel_spmd(nc, in_maps, core_ids=list(range(8)))

    out = np.empty((B, T, H), f32)
    for c in range(8):
        b, h = c >> 1, c & 1
        out[b][:, h * OH:(h + 1) * OH] = res.results[c]["out_half"]
    if DEBUG:
        kernel.debug_results = res.results
    return out
